# version 10 (final)
# baseline (speedup 1.0000x reference)
import numpy as np
from contextlib import ExitStack

import concourse.bass as bass
import concourse.mybir as mybir
from concourse.bass_utils import run_bass_kernel_spmd

F32 = mybir.dt.float32
BF16 = mybir.dt.bfloat16
AF = mybir.ActivationFunctionType
ALU = mybir.AluOpType

T = 2048
D = 1024
NCH = 8
TW = 512
NT = T // TW
DH = 4096
NCH_H = DH // 128

ENGS = ("sync", "tensor", "vector", "scalar", "gpsimd")
ENG_SEM = {"sync": "sq", "tensor": "tt", "vector": "vv", "scalar": "ss", "gpsimd": "gg"}
DMA_SEM = {"sync": "sq", "gpsimd": "gw"}
SEMS = ("sq", "tt", "vv", "ss", "gg", "gw")


class Builder:
    def __init__(self):
        self.entries = {e: [] for e in ENGS}
        self.res = {}

    def op(self, eng, fn, reads=(), writes=(), dma=False, tag="", after_readers=()):
        idx = len(self.entries[eng])
        ent = {"fn": fn, "deps": {}, "inc": bool(dma), "dma": bool(dma),
               "eng": eng, "idx": idx, "tag": tag, "semval": None, "sem": None}
        self.entries[eng].append(ent)

        def edge(peng, pidx):
            if peng == eng and pidx == idx:
                return
            p = self.entries[peng][pidx]
            if peng == eng and not p["dma"]:
                return
            p["inc"] = True
            key = (peng, p["dma"])
            if ent["deps"].get(key, -1) < pidx:
                ent["deps"][key] = pidx

        for r in reads:
            rec = self.res.setdefault(r, {"w": None, "r": []})
            if rec["w"] is not None:
                edge(*rec["w"])
            rec["r"].append((eng, idx))
        for w in writes:
            rec = self.res.setdefault(w, {"w": None, "r": []})
            if rec["w"] is not None:
                edge(*rec["w"])
            for (re_, ri) in rec["r"]:
                edge(re_, ri)
            rec["w"] = (eng, idx)
            rec["r"] = []
        for r in after_readers:
            rec = self.res.get(r)
            if rec:
                for (re_, ri) in rec["r"]:
                    edge(re_, ri)
        return (eng, idx)

    def finalize(self):
        counters = {s: 0 for s in SEMS}
        for eng in ENGS:
            for ent in self.entries[eng]:
                if ent["dma"]:
                    sem = DMA_SEM[eng]
                    counters[sem] += 16
                    ent["sem"], ent["semval"] = sem, counters[sem]
                elif ent["inc"]:
                    sem = ENG_SEM[eng]
                    counters[sem] += 1
                    ent["sem"], ent["semval"] = sem, counters[sem]
        MAXQ = 24
        for eng in ENGS:
            ndma = 0
            for ent in self.entries[eng]:
                waits = []
                for ((peng, pdma), pidx) in ent["deps"].items():
                    p = self.entries[peng][pidx]
                    assert p["semval"] is not None, (ent["tag"], peng, pidx)
                    waits.append((p["sem"], p["semval"]))
                if ent["dma"]:
                    ndma += 1
                    if ndma > MAXQ:
                        waits.append((DMA_SEM[eng], 16 * (ndma - MAXQ)))
                ent["waits"] = sorted(waits)
        self.totals = counters
        return counters

    def simulate(self):
        ptr = {e: 0 for e in ENGS}
        sem = {s: 0 for s in SEMS}
        progress = True
        while progress:
            progress = False
            for e in ENGS:
                while ptr[e] < len(self.entries[e]):
                    ent = self.entries[e][ptr[e]]
                    if all(sem[s] >= v for (s, v) in ent["waits"]):
                        if ent["semval"] is not None:
                            sem[ent["sem"]] += 16 if ent["dma"] else 1
                        ptr[e] += 1
                        progress = True
                    else:
                        break
        stuck = {e: ptr[e] for e in ENGS if ptr[e] < len(self.entries[e])}
        if stuck:
            lines = []
            for e, i in stuck.items():
                ent = self.entries[e][i]
                lines.append(f"{e}[{i}] tag={ent['tag']} waits={ent['waits']}")
            lines.append(f"sems={sem}")
            raise RuntimeError("DEADLOCK in build-time simulation:\n" + "\n".join(lines))


def ap3(dram_ap, r0, nrow, c0, w, p=128):
    a = dram_ap[r0:r0 + nrow, c0:c0 + w]
    return a.rearrange("(kc p) j -> p kc j", p=p)


def build_program():
    nc = bass.Bass("TRN2", target_bir_lowering=False)

    dram = {}

    def din(name, shape):
        dram[name] = nc.dram_tensor(name, list(shape), F32, kind="ExternalInput").ap()
        return dram[name]

    xt = din("xt", [D, T])
    y = nc.dram_tensor("y", [D, T], F32, kind="ExternalOutput").ap()

    wqkv = din("wqkv", [D, 3 * D])
    wg = din("wg", [D, D])
    wo = din("wo", [D, D])
    bgate = din("bgate", [D])
    dp = din("dp", [16])

    alpha = din("alpha", [D])
    fcg = din("fcg", [D, D]); fcg_b = din("fcg_b", [D])
    fcw = din("fcw", [D, D]); fcw_b = din("fcw_b", [D])
    fcr = din("fcr", [D, D]); fcr_b = din("fcr_b", [D])
    fci = din("fci", [D, D]); fci_b = din("fci_b", [D])
    nw = din("nw", [D]); nw2 = din("nw2", [D])
    mgw = din("mgw", [D, 3 * D]); mg_b = din("mg_b", [3 * D])
    mfw = din("mfw", [D, 3 * D]); mf_b = din("mf_b", [3 * D])
    oww = din("oww", [3 * D, D]); ow_b = din("ow_b", [D])

    w1 = din("w1", [D, DH]); s1 = din("s1", [DH])
    w2 = din("w2", [D, DH]); s2 = din("s2", [DH])
    h1 = din("h1", [DH, DH]); sh1 = din("sh1", [DH])
    h2 = din("h2", [DH, DH]); sh2 = din("sh2", [DH])
    w3 = din("w3", [DH, D]); s3 = din("s3", [D])

    B = Builder()
    op = B.op
    ctx = ExitStack()

    def sbt(name, cols, dtype=F32):
        return ctx.enter_context(nc.sbuf_tensor(name, [128, cols], dtype))

    XR = sbt("XR", NCH * T)
    PA = sbt("PA", 16384)
    PB = sbt("PB", 6144)
    WBUF = sbt("WBUF", 2048)
    XNB = sbt("XNB", 2048)
    CB = sbt("CB", 2048)
    SS = sbt("SS", 3072)
    CONSTS = sbt("CONSTS", 512)
    ONESB = sbt("ONESB", 128, BF16)
    ONESF = sbt("ONESF", 128)
    BLK1 = sbt("BLK1", 128)
    ONES5 = sbt("ONES5", TW)
    DBC = sbt("DBC", 2 * TW)
    STC = sbt("STC", 64)

    PS = [ctx.enter_context(nc.psum_tensor(f"ps{i}", [128, TW], F32)) for i in range(8)]
    sems = {s: ctx.enter_context(nc.semaphore(s)) for s in SEMS}

    PAb = PA[:].bitcast(BF16)
    PBb = PB[:].bitcast(BF16)
    WBb = WBUF[:].bitcast(BF16)
    XNBb = XNB[:].bitcast(BF16)
    CBb = CB[:].bitcast(BF16)

    M_ = SS[:, 0:512]
    RSTD = SS[:, 512:1024]
    RSTD2 = SS[:, 1024:1536]
    TMPA = SS[:, 1536:2048]
    TMPB = SS[:, 2048:2560]
    TMPC = SS[:, 2560:3072]

    _colptr = [0]
    cols = {}

    def calloc(name, n):
        cols[name] = _colptr[0]
        _colptr[0] += n
        assert _colptr[0] <= 512

    for nm, n in [("eps_ln", 1), ("eps_rms", 1), ("clip_den", 1), ("one", 1),
                  ("zero", 1),
                  ("bgate", 8), ("b_g", 8), ("b_fc", 8), ("b_r", 8), ("b_i", 8),
                  ("nw", 8), ("nw2", 8), ("ow_b", 8), ("alpha", 8), ("msp8", 8),
                  ("dcol", 8), ("mg_b", 24), ("mf_b", 24),
                  ("s1", 32), ("s2", 32), ("s12", 32), ("sh1", 32), ("sh2", 32),
                  ("s3", 8), ("sp_t", 8), ("sp_t2", 8)]:
        calloc(nm, n)

    def ccol(name, j=0):
        a = cols[name] + j
        return CONSTS[:, a:a + 1]

    def crange(name, n):
        a = cols[name]
        return CONSTS[:, a:a + n]

    for nm, val in [("eps_ln", 1e-5), ("eps_rms", 1e-6), ("clip_den", 1e-6),
                    ("one", 1.0), ("zero", 0.0)]:
        op("gpsimd", lambda e, nm=nm, val=val: e.memset(ccol(nm), val),
           writes=[f"C_{nm}"], tag="ms")
    op("gpsimd", lambda e: e.memset(ONESB[:], 1.0), writes=["ONESB"], tag="ms")
    op("gpsimd", lambda e: e.memset(ONESF[:], 1.0), writes=["ONESF"], tag="ms")
    op("gpsimd", lambda e: e.memset(ONES5[:], 1.0), writes=["ONES5"], tag="ms")
    op("gpsimd", lambda e: e.memset(BLK1[:], 0.0), writes=["BLK1"], tag="ms")
    op("gpsimd", lambda e: e.memset(BLK1[0:64, 0:64], 1.0),
       reads=["BLK1"], writes=["BLK1"], tag="ms")
    op("gpsimd", lambda e: e.memset(BLK1[64:128, 64:128], 1.0),
       reads=["BLK1"], writes=["BLK1"], tag="ms")
    op("gpsimd", lambda e: e.memset(STC[:, 0:32], 0.0), writes=["STCZ"], tag="ms")

    op("sync", lambda e: e.dma_start(
        out=XR[:].rearrange("p (c t) -> p c t", t=T),
        in_=xt.rearrange("(c p) t -> p c t", p=128)),
        writes=[f"X{t}_{c}" for t in range(NT) for c in range(NCH)], dma=True, tag="ldx")

    def load_cols(vec_ap, cname, n):
        dst = crange(cname, n)
        op("sync", lambda e, vec_ap=vec_ap, dst=dst: e.dma_start(
            out=dst, in_=vec_ap.rearrange("(c p) -> p c", p=128),
            allow_slow_non_contiguous=True),
            writes=[f"C_{cname}"], dma=True, tag="ldc" + cname)

    for vec_ap, cname in [(bgate, "bgate"), (fcg_b, "b_g"), (fcw_b, "b_fc"),
                          (fcr_b, "b_r"), (fci_b, "b_i"), (nw, "nw"), (nw2, "nw2"),
                          (ow_b, "ow_b"), (alpha, "alpha"), (mg_b, "mg_b"),
                          (mf_b, "mf_b"), (s1, "s1"), (s2, "s2"), (sh1, "sh1"),
                          (sh2, "sh2"), (s3, "s3")]:
        load_cols(vec_ap, cname, vec_ap.shape[0] // 128)

    op("sync", lambda e: e.dma_start(out=STC[0:1, 48:64],
                                     in_=dp.rearrange("(p k) -> p k", p=1)),
       writes=["DPROW"], dma=True, tag="lddp")

    op("gpsimd", lambda e: e.tensor_tensor(crange("s12", 32), crange("s1", 32),
                                           crange("s2", 32), ALU.mult),
       reads=["C_s1", "C_s2"], writes=["C_s12"], tag="s12")

    op("scalar", lambda e: e.activation(crange("sp_t", 8), crange("alpha", 8), AF.Abs),
       reads=["C_alpha"], writes=["C_sp_t"], tag="msp")
    op("scalar", lambda e: e.activation(crange("sp_t", 8), crange("sp_t", 8), AF.Exp,
                                        scale=-1.0),
       reads=["C_sp_t"], writes=["C_sp_t"], tag="msp")
    op("scalar", lambda e: e.activation(crange("sp_t", 8), crange("sp_t", 8), AF.Ln,
                                        bias=ccol("one")),
       reads=["C_sp_t", "C_one"], writes=["C_sp_t"], tag="msp")
    op("vector", lambda e: e.tensor_relu(crange("sp_t2", 8), crange("alpha", 8)),
       reads=["C_alpha"], writes=["C_sp_t2"], tag="msp")
    op("vector", lambda e: e.tensor_add(crange("sp_t2", 8), crange("sp_t2", 8),
                                        crange("sp_t", 8)),
       reads=["C_sp_t", "C_sp_t2"], writes=["C_sp_t2"], tag="msp")
    op("vector", lambda e: e.tensor_scalar(crange("msp8", 8), crange("sp_t2", 8),
                                           -8.0, None, ALU.mult),
       reads=["C_sp_t2"], writes=["C_msp8"], tag="msp")

    op("scalar", lambda e: e.activation(STC[0:1, 48:64], STC[0:1, 48:64], AF.Sigmoid),
       reads=["DPROW"], writes=["DPROW"], tag="dec")
    op("tensor", lambda e: e.matmul(PS[0][:, 0:16], ONESF[0:1, :], STC[0:1, 48:64],
                                    start=True, stop=True),
       reads=["ONESF", "DPROW"], writes=["PS0"], tag="decmm")
    op("vector", lambda e: e.tensor_copy(STC[:, 32:48], PS[0][:, 0:16]),
       reads=["PS0"], writes=["LDB"], tag="decev")
    for c in range(NCH):
        op("scalar", lambda e, c=c: e.copy(
            CONSTS[0:64, cols["dcol"] + c:cols["dcol"] + c + 1],
            STC[0:64, 32 + 2 * c:32 + 2 * c + 1]),
            reads=["LDB"], writes=["C_dcol"], tag="decc")
        op("scalar", lambda e, c=c: e.copy(
            CONSTS[64:128, cols["dcol"] + c:cols["dcol"] + c + 1],
            STC[64:128, 32 + 2 * c + 1:32 + 2 * c + 2]),
            reads=["LDB"], writes=["C_dcol"], tag="decc")

    def xs(t, c):
        return XR[:, c * T + t * TW: c * T + t * TW + TW]

    def xres(t, c):
        return f"X{t}_{c}"

    def pa(slot):
        return PA[:, slot * TW:(slot + 1) * TW]

    def xnb(c):
        return XNBb[:, c * TW:(c + 1) * TW]

    def cbb(c):
        return CBb[:, c * TW:(c + 1) * TW]

    wb_par = [0]

    def wb_next():
        par = wb_par[0] % 4
        wb_par[0] += 1
        return par * 1024, f"WB{par}"

    fw_par = [0]

    def fwb_next():
        par = fw_par[0] % 3
        fw_par[0] += 1
        poff = par * 4096
        return poff, [f"PB{4 * par + i}" for i in range(4)]

    import os
    CACHE_MATS = set(os.environ.get("KCACHE", "wqkv,wg,wo,fcg,fcr,fci,fcw,mgw,mfw,oww,w1,w2,h1,h2,w3").split(","))
    scratch = {}
    pending_stores = []

    def flush_stores():
        while pending_stores:
            pending_stores.pop(0)()

    SCR_TOTAL = 507904
    if os.environ.get("KSCREXT"):
        scr_arena = nc.dram_tensor("scr_arena", [128, SCR_TOTAL], BF16, kind="ExternalOutput").ap()
    else:
        scr_arena = nc.dram_tensor("scr_arena", [128, SCR_TOTAL], BF16).ap()
    scr_ptr = [0]

    def wpanel(mat, wt, k0, nk, c0, w, dst, pres_list, tag):
        if mat is not None and mat not in CACHE_MATS:
            mat = None
        if mat is None:
            src = ap3(wt, k0 * 128, nk * 128, c0, w)
            op("gpsimd", lambda e, dst=dst, src=src: e.dma_start(out=dst, in_=src),
               writes=pres_list, dma=True, tag=tag + "d")
            return
        if mat not in scratch:
            rows, colsn = wt.shape
            n = rows * colsn // 128
            base = scr_ptr[0]
            scr_ptr[0] += n
            assert scr_ptr[0] <= SCR_TOTAL
            scratch[mat] = {"ap": scr_arena[:, base:base + n],
                            "ptr": 0, "panels": {}}
        sc = scratch[mat]
        key = (k0, nk, c0, w)
        if key not in sc["panels"]:
            off = sc["ptr"]
            sc["ptr"] += nk * w
            sc["panels"][key] = off
            src = ap3(wt, k0 * 128, nk * 128, c0, w)
            op("gpsimd", lambda e, dst=dst, src=src: e.dma_start(out=dst, in_=src),
               writes=pres_list, dma=True, tag=tag + "c")
            scr = sc["ap"][:, off:off + nk * w].rearrange("p (kc j) -> p kc j", j=w)

            def emit_store(scr=scr, dst=dst, pres_list=pres_list, mat=mat, key=key, tag=tag):
                op("sync", lambda e, scr=scr, dst=dst: e.dma_start(out=scr, in_=dst),
                   reads=pres_list, after_readers=pres_list,
                   writes=[f"SC_{mat}_{key}"], dma=True, tag=tag + "s")
            pending_stores.append(emit_store)
        else:
            off = sc["panels"][key]
            scr = sc["ap"][:, off:off + nk * w].rearrange("p (kc j) -> p kc j", j=w)
            if os.environ.get("KNOREREAD"):
                src = ap3(wt, k0 * 128, nk * 128, c0, w)
                op("gpsimd", lambda e, dst=dst, src=src: e.dma_start(out=dst, in_=src),
                   writes=pres_list, dma=True, tag=tag + "d2")
            else:
                op("sync", lambda e, scr=scr, dst=dst: e.dma_start(out=dst, in_=scr),
                   reads=[f"SC_{mat}_{key}"], writes=pres_list, dma=True, tag=tag + "r")

    def layernorm(t, tag, write_xn=True):
        for c in range(NCH):
            op("vector", lambda e, t=t, c=c: e.tensor_copy(xnb(c), xs(t, c)),
               reads=[xres(t, c)], writes=[f"XNB{c}"], tag=tag + "xb")
            op("scalar", lambda e, t=t, c=c: e.activation(cbb(c), xs(t, c), AF.Square),
               reads=[xres(t, c)], writes=[f"CB{c}"], tag=tag + "x2b")
        for c in range(NCH):
            op("tensor", lambda e, c=c: e.matmul(PS[0][:], ONESB[:], xnb(c),
                                                 start=(c == 0), stop=(c == NCH - 1)),
               reads=["ONESB", f"XNB{c}", "PS0"], writes=["PS0"], tag=tag + "mm1")
        for c in range(NCH):
            op("tensor", lambda e, c=c: e.matmul(PS[1][:], ONESB[:], cbb(c),
                                                 start=(c == 0), stop=(c == NCH - 1)),
               reads=["ONESB", f"CB{c}", "PS1"], writes=["PS1"], tag=tag + "mm2")
        op("scalar", lambda e: e.activation(M_, PS[0][:], AF.Copy, scale=1.0 / 1024.0),
           reads=["PS0"], writes=["SS_M"], tag=tag + "m")
        op("scalar", lambda e: e.activation(TMPA, M_, AF.Square),
           reads=["SS_M"], writes=["SS_TA"], tag=tag + "msq")
        op("vector", lambda e: e.scalar_tensor_tensor(
            TMPB, PS[1][:], 1.0 / 1024.0, TMPA, ALU.mult, ALU.subtract),
            reads=["PS1", "SS_TA"], writes=["SS_TB"], tag=tag + "var")
        op("scalar", lambda e: e.activation(TMPA, TMPB, AF.Ln, bias=ccol("eps_ln")),
           reads=["SS_TB", "C_eps_ln"], writes=["SS_TA"], tag=tag + "sd")
        op("scalar", lambda e: e.activation(RSTD, TMPA, AF.Exp, scale=-0.5),
           reads=["SS_TA"], writes=["SS_RSTD"], tag=tag + "rstd")
        if write_xn:
            for c in range(NCH):
                op("vector", lambda e, t=t, c=c: e.tensor_sub(TMPB, xs(t, c), M_),
                   reads=[xres(t, c), "SS_M"], writes=["SS_TB"], tag=tag + "sub")
                op("vector", lambda e, c=c: e.tensor_mul(xnb(c), TMPB, RSTD),
                   reads=["SS_TB", "SS_RSTD"], writes=[f"XNB{c}"], tag=tag + "xn")

    def mm_stage(tag, wt, nkc, col0, n_m, rhs_res, rhs_slice, evac, gm=8, mat=None):
        for m0 in range(0, n_m, gm):
            ms = list(range(m0, min(m0 + gm, n_m)))
            w = len(ms) * 128
            assert w <= 1024
            for k in range(nkc):
                poff, pres = wb_next()
                dst = WBb[:, poff:poff + w].rearrange("p (kc j) -> p kc j", j=w)
                wpanel(mat, wt, k, 1, col0 + m0 * 128, w, dst, [pres], tag + "w")
                for j, m in enumerate(ms):
                    lhsT = WBb[:, poff + j * 128: poff + (j + 1) * 128]
                    bk = j
                    op("tensor", lambda e, lhsT=lhsT, bk=bk, k=k, r=rhs_slice(k):
                       e.matmul(PS[bk][:], lhsT, r, start=(k == 0), stop=(k == nkc - 1)),
                       reads=[pres, rhs_res(k), f"PS{bk}"], writes=[f"PS{bk}"],
                       tag=tag + "mm")
                flush_stores()
            for j, m in enumerate(ms):
                evac(m, j)

    def attn_a(t):
        tag = f"A{t}_"
        layernorm(t, tag + "ln")

        def evac_qkv(m, bk):
            grp, c = divmod(m, NCH)
            if grp < 2:
                slot = c if grp == 0 else 8 + c
                dres = f"PA{slot}"
                tmp, tres = (TMPB, "SS_TB") if c % 2 == 0 else (TMPC, "SS_TC")
                op("vector", lambda e, d=pa(slot), bk=bk: e.tensor_scalar(
                    d, PS[bk][:], 0.0, None, ALU.max),
                    reads=[f"PS{bk}"], writes=[dres], tag=tag + "phi1")
                op("vector", lambda e, tmp=tmp, bk=bk: e.tensor_scalar(
                    tmp, PS[bk][:], 0.0, None, ALU.min),
                    reads=[f"PS{bk}"], writes=[tres], tag=tag + "phi2")
                op("scalar", lambda e, tmp=tmp: e.activation(tmp, tmp, AF.Exp),
                   reads=[tres], writes=[tres], tag=tag + "phi3")
                op("vector", lambda e, d=pa(slot), tmp=tmp: e.tensor_add(d, d, tmp),
                   reads=[tres, dres], writes=[dres], tag=tag + "phi4")
            else:
                op("scalar", lambda e, d=pa(16 + c), bk=bk: e.copy(d, PS[bk][:]),
                   reads=[f"PS{bk}"], writes=[f"PA{16 + c}"], tag=tag + "vev")

        mm_stage(tag + "qkv", wqkv, NCH, 0, 24,
                 lambda k: f"XNB{k}", xnb, evac_qkv, mat="wqkv")

        for c in range(NCH):
            op("vector", lambda e, c=c: e.tensor_mul(pa(24 + c), pa(8 + c), pa(16 + c)),
               reads=[f"PA{8 + c}", f"PA{16 + c}"], writes=[f"PA{24 + c}"], tag=tag + "kv")

        def dbcast(c):
            return bass.AP(CONSTS, cols["dcol"] + c, [[512, 128], [0, TW]])

        for c in range(NCH):
            init = ccol("zero") if t == 0 else STC[:, c:c + 1]
            ires = "C_zero" if t == 0 else f"STKV{c}"
            op("vector", lambda e, c=c, init=init: e.tensor_tensor_scan(
                PB[:, c * TW:(c + 1) * TW], dbcast(c),
                pa(24 + c), init, ALU.mult, ALU.add),
                reads=["C_dcol", f"PA{24 + c}", ires, f"PB{c}"],
                writes=[f"PB{c}"], tag=tag + "sckv")
            if t < NT - 1:
                op("gpsimd", lambda e, c=c: e.tensor_copy(
                    STC[:, c:c + 1], PB[:, c * TW + TW - 1:c * TW + TW]),
                    reads=[f"PB{c}"], writes=[f"STKV{c}"], tag=tag + "stkv")
        for c in range(NCH):
            init = ccol("zero") if t == 0 else STC[:, 8 + c:8 + c + 1]
            ires = "C_zero" if t == 0 else f"STKS{c}"
            op("vector", lambda e, c=c, init=init: e.tensor_tensor_scan(
                pa(24 + c), dbcast(c),
                pa(8 + c), init, ALU.mult, ALU.add),
                reads=["C_dcol", f"PA{8 + c}", ires, f"PA{24 + c}"],
                writes=[f"PA{24 + c}"], tag=tag + "scks")
            if t < NT - 1:
                op("gpsimd", lambda e, c=c: e.tensor_copy(
                    STC[:, 8 + c:8 + c + 1], pa(24 + c)[:, TW - 1:TW]),
                    reads=[f"PA{24 + c}"], writes=[f"STKS{c}"], tag=tag + "stks")

        for c in range(NCH):
            bk = 2 + (c % 2)
            op("vector", lambda e, c=c: e.tensor_mul(TMPB, pa(c), pa(24 + c)),
               reads=[f"PA{c}", f"PA{24 + c}"], writes=["SS_TB"], tag=tag + "qk")
            op("tensor", lambda e, bk=bk: e.matmul(PS[bk][:], BLK1[:], TMPB,
                                                   start=True, stop=True),
               reads=["BLK1", "SS_TB", f"PS{bk}"], writes=[f"PS{bk}"], tag=tag + "denmm")
            op("vector", lambda e, c=c, bk=bk: e.tensor_scalar(
                pa(8 + c), PS[bk][:], ccol("clip_den"), None, ALU.max),
                reads=[f"PS{bk}", "C_clip_den", f"PA{8 + c}"], writes=[f"PA{8 + c}"],
                tag=tag + "dmax")
            tmp, tres = (TMPA, "SS_TA") if c % 2 == 0 else (TMPC, "SS_TC")
            op("scalar", lambda e, c=c, tmp=tmp: e.activation(tmp, pa(8 + c), AF.Ln),
               reads=[f"PA{8 + c}"], writes=[tres], tag=tag + "dln")
            op("scalar", lambda e, c=c, tmp=tmp: e.activation(pa(8 + c), tmp, AF.Exp,
                                                              scale=-1.0),
               reads=[tres, f"PA{8 + c}"], writes=[f"PA{8 + c}"], tag=tag + "dexp")

        for c in range(NCH):
            op("vector", lambda e, c=c: e.tensor_mul(pa(c), pa(c), PB[:, c * TW:(c + 1) * TW]),
               reads=[f"PA{c}", f"PB{c}"], writes=[f"PA{c}"], tag=tag + "at1")
            op("vector", lambda e, c=c: e.tensor_mul(pa(c), pa(c), pa(8 + c)),
               reads=[f"PA{c}", f"PA{8 + c}"], writes=[f"PA{c}"], tag=tag + "at2")
            op("scalar", lambda e, c=c: e.copy(cbb(c), pa(c)),
               reads=[f"PA{c}"], writes=[f"CB{c}"], tag=tag + "atb")

        def evac_g(m, bk):
            op("scalar", lambda e, m=m, bk=bk: e.activation(
                PB[:, m * TW:(m + 1) * TW], PS[bk][:], AF.Sigmoid, bias=ccol("bgate", m)),
                reads=[f"PS{bk}", "C_bgate"], writes=[f"PB{m}"], tag=tag + "gev")

        mm_stage(tag + "g", wg, NCH, 0, NCH, lambda k: f"CB{k}", cbb, evac_g, mat="wg")

        for c in range(NCH):
            op("vector", lambda e, c=c: e.tensor_sub(pa(c), pa(c), pa(16 + c)),
               reads=[f"PA{c}", f"PA{16 + c}"], writes=[f"PA{c}"], tag=tag + "gd1")
            op("vector", lambda e, c=c: e.tensor_mul(pa(c), pa(c), PB[:, c * TW:(c + 1) * TW]),
               reads=[f"PA{c}", f"PB{c}"], writes=[f"PA{c}"], tag=tag + "gd2")
            op("vector", lambda e, c=c: e.tensor_add(xnb(c), pa(c), pa(16 + c)),
               reads=[f"PA{c}", f"PA{16 + c}"], writes=[f"XNB{c}"], tag=tag + "gd3")

        def evac_o(m, bk):
            op("vector", lambda e, t=t, m=m, bk=bk: e.tensor_add(
                xs(t, m), PS[bk][:], xs(t, m)),
                reads=[f"PS{bk}", xres(t, m)], writes=[xres(t, m)], tag=tag + "oev")

        mm_stage(tag + "o", wo, NCH, 0, NCH, lambda k: f"XNB{k}", xnb, evac_o, mat="wo")

    def ffn(t, pfx):
        tag = f"{pfx}{t}_"
        layernorm(t, tag + "ln")

        def hb(kk):
            return PAb[:, kk * TW:(kk + 1) * TW]

        def pbch(kk):
            return PAb[:, 16384 + kk * TW: 16384 + (kk + 1) * TW]

        def hbres(kk):
            return f"PA{kk // 2}"

        def pbres(kk):
            return f"PA{16 + kk // 2}"

        for m0 in range(0, NCH_H, 4):
            poff1, pres1 = fwb_next()
            dst1 = PBb[:, poff1:poff1 + 8 * TW].rearrange("p (kc j) -> p kc j", j=TW)
            wpanel("w1", w1, 0, NCH, m0 * 128, TW, dst1, pres1, tag + "w1")
            poff2, pres2 = fwb_next()
            dst2 = PBb[:, poff2:poff2 + 8 * TW].rearrange("p (kc j) -> p kc j", j=TW)
            wpanel("w2", w2, 0, NCH, m0 * 128, TW, dst2, pres2, tag + "w2")
            for k in range(NCH):
                for j in range(4):
                    l1 = PBb[:, poff1 + k * TW + j * 128: poff1 + k * TW + (j + 1) * 128]
                    l2 = PBb[:, poff2 + k * TW + j * 128: poff2 + k * TW + (j + 1) * 128]
                    op("tensor", lambda e, l=l1, b=2 * j, k=k: e.matmul(
                        PS[b][:], l, xnb(k), start=(k == 0), stop=(k == NCH - 1)),
                        reads=pres1 + [f"XNB{k}", f"PS{2 * j}"], writes=[f"PS{2 * j}"],
                        tag=tag + "s1mm")
                    op("tensor", lambda e, l=l2, b=2 * j + 1, k=k: e.matmul(
                        PS[b][:], l, xnb(k), start=(k == 0), stop=(k == NCH - 1)),
                        reads=pres2 + [f"XNB{k}", f"PS{2 * j + 1}"],
                        writes=[f"PS{2 * j + 1}"], tag=tag + "s1mm")
            flush_stores()
            for j in range(4):
                m = m0 + j
                b1, b2 = 2 * j, 2 * j + 1
                tmp, tres = (TMPA, "SS_TA") if j % 2 == 0 else (TMPC, "SS_TC")
                op("scalar", lambda e, tmp=tmp, b=b1, m=m: e.activation(
                    tmp, PS[b][:], AF.Sigmoid, scale=ccol("s1", m)),
                    reads=[f"PS{b1}", "C_s1"], writes=[tres], tag=tag + "s1sg")
                op("vector", lambda e, tmp=tmp, b=b1: e.tensor_mul(TMPB, PS[b][:], tmp),
                   reads=[f"PS{b1}", tres], writes=["SS_TB"], tag=tag + "s1t")
                op("vector", lambda e, m=m, b=b2: e.scalar_tensor_tensor(
                    hb(m), TMPB, ccol("s12", m), PS[b][:], ALU.mult, ALU.mult),
                    reads=["SS_TB", "C_s12", f"PS{b2}", hbres(m)], writes=[hbres(m)],
                    tag=tag + "s1h")

        def hstage(wt, mat, scol, src_ch, dst_ch, src_res, dst_res, tg):
            for m0 in range(0, NCH_H, 8):
                for kb in range(0, NCH_H, 4):
                    poff, pres = fwb_next()
                    w = 1024
                    dst = PBb[:, poff:poff + 4 * w].rearrange("p (kc j) -> p kc j", j=w)
                    wpanel(mat, wt, kb, 4, m0 * 128, w, dst, pres, tg + "w")
                    for kloc in range(4):
                        k = kb + kloc
                        for j in range(8):
                            l = PBb[:, poff + kloc * w + j * 128:
                                    poff + kloc * w + (j + 1) * 128]
                            op("tensor", lambda e, l=l, b=j, k=k, r=src_ch(k): e.matmul(
                                PS[b][:], l, r, start=(k == 0), stop=(k == NCH_H - 1)),
                                reads=pres + [src_res(k), f"PS{j}"], writes=[f"PS{j}"],
                                tag=tg + "mm")
                    flush_stores()
                for j in range(8):
                    m = m0 + j
                    op("scalar", lambda e, m=m, b=j: e.activation(
                        dst_ch(m), PS[b][:], AF.Gelu, scale=ccol(scol, m)),
                        reads=[f"PS{j}", f"C_{scol}", dst_res(m)], writes=[dst_res(m)],
                        tag=tg + "ev")

        hstage(h1, "h1", "sh1", hb, pbch, hbres, pbres, tag + "h1")
        hstage(h2, "h2", "sh2", pbch, hb, pbres, hbres, tag + "h2")

        for kb in range(0, NCH_H, 4):
            poff, pres = fwb_next()
            w = 1024
            dst = PBb[:, poff:poff + 4 * w].rearrange("p (kc j) -> p kc j", j=w)
            wpanel("w3", w3, kb, 4, 0, w, dst, pres, tag + "w3w")
            for kloc in range(4):
                k = kb + kloc
                for j in range(8):
                    l = PBb[:, poff + kloc * w + j * 128: poff + kloc * w + (j + 1) * 128]
                    op("tensor", lambda e, l=l, b=j, k=k: e.matmul(
                        PS[b][:], l, hb(k), start=(k == 0), stop=(k == NCH_H - 1)),
                        reads=pres + [hbres(k), f"PS{j}"], writes=[f"PS{j}"],
                        tag=tag + "w3mm")
            flush_stores()
        for j in range(8):
            op("vector", lambda e, t=t, m=j: e.scalar_tensor_tensor(
                xs(t, m), PS[m][:], ccol("s3", m), xs(t, m), ALU.mult, ALU.add),
                reads=[f"PS{j}", "C_s3", xres(t, j)], writes=[xres(t, j)],
                tag=tag + "w3ev")

    def attn_b(t):
        tag = f"B{t}_"
        layernorm(t, tag + "ln", write_xn=False)

        for c in range(NCH):
            op("vector", lambda e, t=t, c=c: e.tensor_sub(TMPB, xs(t, c), M_),
               reads=[xres(t, c), "SS_M"], writes=["SS_TB"], tag=tag + "h1a")
            op("vector", lambda e: e.tensor_mul(TMPB, TMPB, RSTD),
               reads=["SS_TB", "SS_RSTD"], writes=["SS_TB"], tag=tag + "h1b")
            op("scalar", lambda e, c=c: e.activation(cbb(c), TMPB, AF.Square),
               reads=["SS_TB"], writes=[f"CB{c}"], tag=tag + "h1sq")
        for c in range(NCH):
            op("tensor", lambda e, c=c: e.matmul(PS[0][:], ONESB[:], cbb(c),
                                                 start=(c == 0), stop=(c == NCH - 1)),
               reads=["ONESB", f"CB{c}", "PS0"], writes=["PS0"], tag=tag + "r1mm")
        op("scalar", lambda e: e.activation(TMPA, PS[0][:], AF.Ln,
                                            scale=1.0 / 1024.0, bias=ccol("eps_rms")),
           reads=["PS0", "C_eps_rms"], writes=["SS_TA"], tag=tag + "r1sd")
        op("scalar", lambda e: e.activation(RSTD2, TMPA, AF.Exp, scale=-0.5),
           reads=["SS_TA"], writes=["SS_R2"], tag=tag + "r1r")
        for c in range(NCH):
            op("vector", lambda e, t=t, c=c: e.tensor_sub(TMPB, xs(t, c), M_),
               reads=[xres(t, c), "SS_M"], writes=["SS_TB"], tag=tag + "hna")
            op("vector", lambda e: e.tensor_mul(TMPB, TMPB, RSTD),
               reads=["SS_TB", "SS_RSTD"], writes=["SS_TB"], tag=tag + "hnb")
            op("vector", lambda e: e.tensor_mul(TMPB, TMPB, RSTD2),
               reads=["SS_TB", "SS_R2"], writes=["SS_TB"], tag=tag + "hnc")
            op("vector", lambda e, c=c: e.tensor_scalar(
                xnb(c), TMPB, ccol("nw", c), None, ALU.mult),
                reads=["SS_TB", "C_nw"], writes=[f"XNB{c}"], tag=tag + "hnd")

        def evac_gate(m, bk):
            op("scalar", lambda e, m=m, bk=bk: e.activation(
                pa(m), PS[bk][:], AF.Gelu, bias=ccol("b_g", m)),
                reads=[f"PS{bk}", "C_b_g", f"PA{m}"], writes=[f"PA{m}"], tag=tag + "gev")

        def evac_a(m, bk):
            tmp, tres = (TMPA, "SS_TA") if m % 2 == 0 else (TMPC, "SS_TC")
            op("scalar", lambda e, m=m, bk=bk, tmp=tmp: e.activation(
                tmp, PS[bk][:], AF.Sigmoid, bias=ccol("b_r", m)),
                reads=[f"PS{bk}", "C_b_r"], writes=[tres], tag=tag + "aev1")
            op("scalar", lambda e, m=m, tmp=tmp: e.activation(
                pa(8 + m), tmp, AF.Exp, scale=ccol("msp8", m)),
                reads=[tres, "C_msp8", f"PA{8 + m}"], writes=[f"PA{8 + m}"],
                tag=tag + "aev2")

        def evac_i(m, bk):
            op("scalar", lambda e, m=m, bk=bk: e.activation(
                pa(16 + m), PS[bk][:], AF.Sigmoid, bias=ccol("b_i", m)),
                reads=[f"PS{bk}", "C_b_i", f"PA{16 + m}"], writes=[f"PA{16 + m}"],
                tag=tag + "iev")

        def evac_hs(m, bk):
            op("vector", lambda e, m=m, bk=bk: e.tensor_scalar(
                pa(24 + m), PS[bk][:], ccol("b_fc", m), None, ALU.add),
                reads=[f"PS{bk}", "C_b_fc", f"PA{24 + m}"], writes=[f"PA{24 + m}"],
                tag=tag + "hsev")

        rr = lambda k: f"XNB{k}"
        mm_stage(tag + "g", fcg, NCH, 0, NCH, rr, xnb, evac_gate, mat="fcg")
        mm_stage(tag + "r", fcr, NCH, 0, NCH, rr, xnb, evac_a, mat="fcr")
        mm_stage(tag + "i", fci, NCH, 0, NCH, rr, xnb, evac_i, mat="fci")
        mm_stage(tag + "hs", fcw, NCH, 0, NCH, rr, xnb, evac_hs, mat="fcw")

        for c in range(NCH):
            tmp, tres = (TMPA, "SS_TA") if c % 2 == 0 else (TMPC, "SS_TC")
            op("scalar", lambda e, c=c, tmp=tmp: e.activation(tmp, pa(8 + c), AF.Square),
               reads=[f"PA{8 + c}"], writes=[tres], tag=tag + "s1")
            op("scalar", lambda e, tmp=tmp: e.activation(tmp, tmp, AF.Sqrt,
                                                         scale=-1.0, bias=ccol("one")),
               reads=[tres, "C_one"], writes=[tres], tag=tag + "s2")
            op("vector", lambda e, c=c, tmp=tmp: e.tensor_mul(pa(24 + c), pa(24 + c), tmp),
               reads=[f"PA{24 + c}", tres], writes=[f"PA{24 + c}"], tag=tag + "s3")
            op("vector", lambda e, c=c: e.tensor_mul(pa(24 + c), pa(24 + c), pa(16 + c)),
               reads=[f"PA{24 + c}", f"PA{16 + c}"], writes=[f"PA{24 + c}"],
               tag=tag + "s4")

        for c in range(NCH):
            init = ccol("zero") if t == 0 else STC[:, 16 + c:16 + c + 1]
            ires = "C_zero" if t == 0 else f"STB{c}"
            op("vector", lambda e, c=c, init=init: e.tensor_tensor_scan(
                pa(16 + c), pa(8 + c), pa(24 + c), init, ALU.mult, ALU.add),
                reads=[f"PA{8 + c}", f"PA{24 + c}", ires, f"PA{16 + c}"],
                writes=[f"PA{16 + c}"], tag=tag + "scan")
            if t < NT - 1:
                op("gpsimd", lambda e, c=c: e.tensor_copy(
                    STC[:, 16 + c:16 + c + 1], pa(16 + c)[:, TW - 1:TW]),
                    reads=[f"PA{16 + c}"], writes=[f"STB{c}"], tag=tag + "stb")

        for c in range(NCH):
            op("vector", lambda e, t=t, c=c: e.tensor_sub(TMPB, xs(t, c), M_),
               reads=[xres(t, c), "SS_M"], writes=["SS_TB"], tag=tag + "q1")
            op("vector", lambda e: e.tensor_mul(TMPB, TMPB, RSTD),
               reads=["SS_TB", "SS_RSTD"], writes=["SS_TB"], tag=tag + "q2")
            op("vector", lambda e, c=c: e.tensor_mul(pa(c), pa(16 + c), pa(c)),
               reads=[f"PA{16 + c}", f"PA{c}"], writes=[f"PA{c}"], tag=tag + "q3")
            op("vector", lambda e, c=c: e.tensor_add(pa(c), pa(c), TMPB),
               reads=[f"PA{c}", "SS_TB"], writes=[f"PA{c}"], tag=tag + "q4")

        for c in range(NCH):
            op("scalar", lambda e, c=c: e.activation(cbb(c), pa(c), AF.Square),
               reads=[f"PA{c}"], writes=[f"CB{c}"], tag=tag + "r2sq")
        for c in range(NCH):
            op("tensor", lambda e, c=c: e.matmul(PS[0][:], ONESB[:], cbb(c),
                                                 start=(c == 0), stop=(c == NCH - 1)),
               reads=["ONESB", f"CB{c}", "PS0"], writes=["PS0"], tag=tag + "r2mm")
        op("scalar", lambda e: e.activation(TMPA, PS[0][:], AF.Ln,
                                            scale=1.0 / 1024.0, bias=ccol("eps_rms")),
           reads=["PS0", "C_eps_rms"], writes=["SS_TA"], tag=tag + "r2sd")
        op("scalar", lambda e: e.activation(RSTD2, TMPA, AF.Exp, scale=-0.5),
           reads=["SS_TA"], writes=["SS_R2"], tag=tag + "r2r")
        for c in range(NCH):
            op("vector", lambda e, c=c: e.tensor_mul(TMPB, pa(c), RSTD2),
               reads=[f"PA{c}", "SS_R2"], writes=["SS_TB"], tag=tag + "r2a")
            op("vector", lambda e, c=c: e.tensor_scalar(
                xnb(c), TMPB, ccol("nw2", c), None, ALU.mult),
                reads=["SS_TB", "C_nw2"], writes=[f"XNB{c}"], tag=tag + "r2b")

        for m0 in range(0, 24, 4):
            for kb in range(0, NCH, 2):
                poffg, presg = wb_next()
                dstg = WBb[:, poffg:poffg + 1024].rearrange("p (kc j) -> p kc j", j=TW)
                wpanel("mgw", mgw, kb, 2, m0 * 128, TW, dstg, [presg], tag + "mgw")
                pofff, presf = wb_next()
                dstf = WBb[:, pofff:pofff + 1024].rearrange("p (kc j) -> p kc j", j=TW)
                wpanel("mfw", mfw, kb, 2, m0 * 128, TW, dstf, [presf], tag + "mfw")
                for kloc in range(2):
                    k = kb + kloc
                    for j in range(4):
                        lg = WBb[:, poffg + kloc * TW + j * 128:
                                 poffg + kloc * TW + (j + 1) * 128]
                        lf = WBb[:, pofff + kloc * TW + j * 128:
                                 pofff + kloc * TW + (j + 1) * 128]
                        op("tensor", lambda e, l=lg, b=2 * j, k=k: e.matmul(
                            PS[b][:], l, xnb(k), start=(k == 0), stop=(k == NCH - 1)),
                            reads=[presg, f"XNB{k}", f"PS{2 * j}"],
                            writes=[f"PS{2 * j}"], tag=tag + "mmg")
                        op("tensor", lambda e, l=lf, b=2 * j + 1, k=k: e.matmul(
                            PS[b][:], l, xnb(k), start=(k == 0), stop=(k == NCH - 1)),
                            reads=[presf, f"XNB{k}", f"PS{2 * j + 1}"],
                            writes=[f"PS{2 * j + 1}"], tag=tag + "mmf")
                flush_stores()
            for j in range(4):
                m = m0 + j
                tmp, tres = (TMPA, "SS_TA") if j % 2 == 0 else (TMPC, "SS_TC")
                op("scalar", lambda e, m=m, b=2 * j, tmp=tmp: e.activation(
                    tmp, PS[b][:], AF.Gelu, bias=ccol("mg_b", m)),
                    reads=[f"PS{2 * j}", "C_mg_b"], writes=[tres], tag=tag + "g2")
                op("vector", lambda e, m=m, b=2 * j + 1, tmp=tmp: e.scalar_tensor_tensor(
                    PBb[:, m * TW:(m + 1) * TW], PS[b][:], ccol("mf_b", m), tmp,
                    ALU.add, ALU.mult),
                    reads=[f"PS{2 * j + 1}", "C_mf_b", tres, f"PB{m // 2}"],
                    writes=[f"PB{m // 2}"], tag=tag + "fg")

        for half in range(2):
            for kb in range(0, 24, 2):
                poff, pres = wb_next()
                dst = WBb[:, poff:poff + 1024].rearrange("p (kc j) -> p kc j", j=TW)
                wpanel("oww", oww, kb, 2, half * TW, TW, dst, [pres], tag + f"oww{half}")
                for kloc in range(2):
                    k = kb + kloc
                    for j in range(4):
                        l = WBb[:, poff + kloc * TW + j * 128:
                                poff + kloc * TW + (j + 1) * 128]
                        b = 4 * half + j
                        op("tensor", lambda e, l=l, b=b, k=k: e.matmul(
                            PS[b][:], l, PBb[:, k * TW:(k + 1) * TW],
                            start=(k == 0), stop=(k == 23)),
                            reads=[pres, f"PB{k // 2}", f"PS{b}"], writes=[f"PS{b}"],
                            tag=tag + "omm")
                flush_stores()
            for j in range(4):
                m = 4 * half + j
                b = 4 * half + j
                op("vector", lambda e, m=m, b=b: e.scalar_tensor_tensor(
                    TMPB, PS[b][:], ccol("ow_b", m), pa(m), ALU.add, ALU.add),
                    reads=[f"PS{b}", "C_ow_b", f"PA{m}"], writes=["SS_TB"],
                    tag=tag + "oe1")
                op("vector", lambda e, t=t, m=m: e.tensor_add(xs(t, m), xs(t, m), TMPB),
                   reads=["SS_TB", xres(t, m)], writes=[xres(t, m)], tag=tag + "oe2")

    if os.environ.get("KSEQ"):
        for t in range(NT):
            attn_a(t)
        for t in range(NT):
            ffn(t, "F")
        for t in range(NT):
            attn_b(t)
        for t in range(NT):
            ffn(t, "G")
    else:
        attn_a(0)
        attn_a(1)
        ffn(0, "F")
        attn_a(2)
        ffn(1, "F")
        attn_a(3)
        ffn(2, "F")
        attn_b(0)
        ffn(3, "F")
        attn_b(1)
        ffn(0, "G")
        attn_b(2)
        ffn(1, "G")
        attn_b(3)
        ffn(2, "G")
        ffn(3, "G")

    flush_stores()
    for t in range(NT):
        for c in range(NCH):
            dst = y[c * 128:(c + 1) * 128, t * TW:(t + 1) * TW].rearrange(
                "(kc p) j -> p kc j", p=128)
            op("sync", lambda e, t=t, c=c, dst=dst: e.dma_start(out=dst, in_=xs(t, c)),
               reads=[xres(t, c)], dma=True, tag=f"sty{t}_{c}")

    totals = B.finalize()
    B.simulate()
    sq_total = totals["sq"]

    with ctx:
        with nc.Block() as block:
            def make_runner(eng_name):
                def go(e):
                    cache = {}
                    for ent in B.entries[eng_name]:
                        for (s, v) in ent["waits"]:
                            if cache.get(s, 0) < v:
                                e.wait_ge(sems[s], v)
                                cache[s] = v
                        ins = ent["fn"](e)
                        if ent["semval"] is not None:
                            ins.then_inc(sems[ent["sem"]], 16 if ent["dma"] else 1)
                    if eng_name == "sync":
                        e.wait_ge(sems["sq"], sq_total)
                return go

            block.sync(make_runner("sync"))
            block.tensor(make_runner("tensor"))
            block.vector(make_runner("vector"))
            block.scalar(make_runner("scalar"))
            block.gpsimd(make_runner("gpsimd"))

    return nc


_cache = {}


def kernel(**inputs):
    xfull = np.asarray(inputs["L0_attn_a_x"], dtype=np.float32)

    tr = lambda a: np.ascontiguousarray(np.asarray(a, dtype=np.float32).T)
    fl = lambda a: np.ascontiguousarray(np.asarray(a, dtype=np.float32).reshape(-1))

    wmap = {
        "wqkv": tr(inputs["L0_attn_a_Wqkv"]),
        "wg": tr(inputs["L0_attn_a_Wgate"]),
        "wo": tr(inputs["L0_attn_a_Wout"]),
        "bgate": fl(inputs["L0_attn_a_bgate"]),
        "dp": fl(inputs["L0_attn_a_decay_param"]),
        "alpha": fl(inputs["L0_attn_b_alpha"]),
        "fcg": tr(inputs["L0_attn_b_fc_gate_w"]), "fcg_b": fl(inputs["L0_attn_b_fc_gate_b"]),
        "fcw": tr(inputs["L0_attn_b_fc_w"]), "fcw_b": fl(inputs["L0_attn_b_fc_b"]),
        "fcr": tr(inputs["L0_attn_b_fc_r_w"]), "fcr_b": fl(inputs["L0_attn_b_fc_r_b"]),
        "fci": tr(inputs["L0_attn_b_fc_i_w"]), "fci_b": fl(inputs["L0_attn_b_fc_i_b"]),
        "nw": fl(inputs["L0_attn_b_norm_w"]), "nw2": fl(inputs["L0_attn_b_norm2_w"]),
        "mgw": tr(inputs["L0_attn_b_mlp_gate_w"]), "mg_b": fl(inputs["L0_attn_b_mlp_gate_b"]),
        "mfw": tr(inputs["L0_attn_b_mlp_fc_w"]), "mf_b": fl(inputs["L0_attn_b_mlp_fc_b"]),
        "oww": tr(inputs["L0_attn_b_out_w"]), "ow_b": fl(inputs["L0_attn_b_out_b"]),
        "w1": tr(inputs["L0_ffn_w1_w"]), "s1": fl(inputs["L0_ffn_w1_s"]),
        "w2": tr(inputs["L0_ffn_w2_w"]), "s2": fl(inputs["L0_ffn_w2_s"]),
        "h1": tr(inputs["L0_ffn_h1_w"]), "sh1": fl(inputs["L0_ffn_h1_s"]),
        "h2": tr(inputs["L0_ffn_h2_w"]), "sh2": fl(inputs["L0_ffn_h2_s"]),
        "w3": tr(inputs["L0_ffn_w3_w"]), "s3": fl(inputs["L0_ffn_w3_s"]),
    }

    if "nc" not in _cache:
        _cache["nc"] = build_program()
    nc = _cache["nc"]

    in_maps = []
    for c in range(8):
        m = dict(wmap)
        m["xt"] = np.ascontiguousarray(xfull[c].T)
        in_maps.append(m)

    res = run_bass_kernel_spmd(nc, in_maps, core_ids=list(range(8)))
    out = np.empty((8, T, D), dtype=np.float32)
    for c in range(8):
        out[c] = np.asarray(res.results[c]["y"]).T
    return out


if __name__ == "__main__":
    import sys
    import time
    t0 = time.time()
    nc = build_program()
    print(f"build OK in {time.time() - t0:.1f}s")
